# version 05
# baseline (speedup 1.0000x reference)
import numpy as np
import ml_dtypes
import concourse.bass as bass
import concourse.mybir as mybir
from concourse.bass_utils import run_bass_kernel_spmd

F32 = mybir.dt.float32
BF16 = mybir.dt.bfloat16
AF = mybir.ActivationFunctionType
ALU = mybir.AluOpType
AX = mybir.AxisListType

D = 2048
NH = 16
C = 128
T = 1024
NCH = 8
HIDP = 5632
HID = 5461
FI = 8192
EPS = 1e-5
C2048 = float(1.0 / np.sqrt(np.float64(2048.0)))
C8192 = float(1.0 / np.sqrt(np.float64(8192.0)))
GAMMAS = [float(1.0 - 2.0 ** (-5.0 - h)) for h in range(NH)]

_CACHE = {}


class Sem:
    def __init__(self, h):
        self.h = h
        self.n = 0


class Buf:
    __slots__ = ("w", "r")

    def __init__(self):
        self.w = None
        self.r = []


class Prog:
    def __init__(self):
        self.sched = []

    def emit(self, eng, sem, fn, reads=(), writes=(), waits=(), dq=16):
        wl = []
        for b in reads:
            if b.w is not None:
                wl.append(b.w)
        for b in writes:
            if b.w is not None:
                wl.append(b.w)
            wl.extend(b.r)
        wl.extend(w for w in waits if w is not None)
        best = {}
        for (s, v) in wl:
            if s not in best or v > best[s]:
                best[s] = v
        wl = [(s, v) for (s, v) in best.items()]
        sem.n += dq if eng == "sync" else 1
        tok = (sem, sem.n)
        self.sched.append((eng, fn, wl, (sem, dq if eng == "sync" else 1)))
        for b in reads:
            b.r.append(tok)
        for b in writes:
            b.w = tok
            b.r = []
        return tok

    def barrier(self, eng, waits):
        self.sched.append((eng, None, list(waits), None))

    def run_engine(self, eng_name, eng):
        for (e, fn, waits, inc) in self.sched:
            if e != eng_name:
                continue
            for (s, v) in waits:
                eng.wait_ge(s.h, v)
            if fn is None:
                continue
            ins = fn(eng)
            if inc is not None:
                ins.then_inc(inc[0].h, inc[1])


def build_nc(dbg=None):
    nc = bass.Bass("TRN2", target_bir_lowering=False)

    def din(name, shape, dt=F32):
        return nc.dram_tensor(name, shape, dt, kind="ExternalInput").ap()

    xo = din("xo", [T, D])
    xp = din("xp", [T, D])
    WQ = din("wq", [D, D]); WK = din("wk", [D, D]); WV = din("wv", [D, D])
    WG = din("wg", [D, D]); WO = din("wo_", [D, D])
    W1 = din("w1p", [D, HIDP]); W2 = din("w2p", [D, HIDP]); W3 = din("w3p", [HIDP, D])
    WGT = din("wgT", [D, FI]); WUT = din("wuT", [D, FI]); WDT = din("wdT", [FI, D])
    N1W = din("n1w", [128, D]); N1B = din("n1b", [128, D])
    N2W = din("n2w", [128, D]); N2B = din("n2b", [128, D])
    MSKD = din("maskT", [NH, C, C])
    GLD = din("glh", [NH, C, C])
    VDECP = din("vdecp", [128, 128])
    VDECO = din("vdeco", [128, 128])
    GCGD = din("gcg", [128, 128])
    IDEN = din("iden", [128, 128])
    ONE1 = din("one1", [1, 128])
    OCOLB = din("ocolb", [128, 1], BF16)
    Y = nc.dram_tensor("y", [T, D], F32, kind="ExternalOutput").ap()
    M2G = nc.dram_tensor("m2g", [D, FI], BF16, kind="Internal").ap()
    M2U = nc.dram_tensor("m2u", [D, FI], BF16, kind="Internal").ap()
    M2D = nc.dram_tensor("m2d", [FI, D], BF16, kind="Internal").ap()
    NCG = nc.dram_tensor("ncg", [1, FI], F32, kind="Internal").ap()
    NCU = nc.dram_tensor("ncu", [1, FI], F32, kind="Internal").ap()
    NCD = nc.dram_tensor("ncd", [1, D], F32, kind="Internal").ap()

    import contextlib
    es = contextlib.ExitStack()
    with es:
        def sbt(name, shape, dt=F32):
            return es.enter_context(nc.sbuf_tensor(name, shape, dt))
        Z = sbt("zbuf", [128, NCH * D])
        AR = sbt("ar", [128, 20480])
        AB = sbt("ab", [128, 10240], BF16)
        WBt = [sbt(f"wb{i}", [128, 512]) for i in range(4)]
        MBt = [sbt(f"mb{i}", [128, 512], BF16) for i in range(2)]
        NRW = sbt("nrw", [128, D])
        NRB = sbt("nrb", [128, D])
        STT = sbt("stt", [128, 128])
        CST = sbt("cst", [128, 32])
        IDn = sbt("idents", [128, 128])
        ONE = sbt("one1s", [1, 128])
        OCB = sbt("ocolbs", [128, 1], BF16)
        OCF = sbt("ocolfs", [128, 1])
        NEG = sbt("negrow", [1, 1024])
        ps = [es.enter_context(nc.psum_tensor(f"ps{i}", [128, 512], F32)) for i in range(8)]

        dL = Sem(es.enter_context(nc.semaphore("dL")))
        dS = Sem(es.enter_context(nc.semaphore("dS")))
        dC = Sem(es.enter_context(nc.semaphore("dC")))
        dNR = Sem(es.enter_context(nc.semaphore("dNR")))
        dX = Sem(es.enter_context(nc.semaphore("dX")))
        dH2 = Sem(es.enter_context(nc.semaphore("dH2")))
        dN = Sem(es.enter_context(nc.semaphore("dN")))
        dK1 = Sem(es.enter_context(nc.semaphore("dK1")))
        dK2 = Sem(es.enter_context(nc.semaphore("dK2")))
        dW = [Sem(es.enter_context(nc.semaphore(f"dW{i}"))) for i in range(4)]
        dM = [Sem(es.enter_context(nc.semaphore(f"dM{i}"))) for i in range(2)]
        dSM = [Sem(es.enter_context(nc.semaphore(f"dSM{i}"))) for i in range(2)]
        dSN = Sem(es.enter_context(nc.semaphore("dSN")))
        tE = Sem(es.enter_context(nc.semaphore("tE")))
        vE = Sem(es.enter_context(nc.semaphore("vE")))
        sE = Sem(es.enter_context(nc.semaphore("sE")))
        block = es.enter_context(nc.Block())

        P = Prog()

        bZ = [Buf() for _ in range(NCH)]
        bAR = {}
        bPS = [Buf() for _ in range(8)]
        bWB = [Buf() for _ in range(4)]
        bMB = [Buf() for _ in range(2)]
        bSTT = Buf()
        bCST = Buf()
        bNRM = Buf()
        bNEG = Buf()
        bAB1 = Buf()
        bAB8 = Buf()
        bOC = Buf()

        def reg(name):
            if name not in bAR:
                bAR[name] = Buf()
            return bAR[name]

        def ar(off, n):
            return AR[:, off:off + n]

        A_MSK, A_GL = 0, 256
        A_MST = 512
        A_XST = 2560
        A_HFM = 4608
        A_QTM = 6656
        A_KTM = 8704
        A_VTM = 10752
        A_RET = 12800
        A_GG = 14848
        A_QH, A_KH, A_QSC, A_SCT, A_RFM = 16896, 17024, 17152, 17280, 17408
        A_TMP = 17536
        A_DUMP = 18048
        A_VDP, A_VDO, A_GCG = 20352, 20224, 20096
        B_H2FM = 0
        B_Y2 = 8192
        B_H2ST = 16384
        B_PTM = 18432
        B_PFM = 18944
        B_UST = 19456
        F_PP0, F_PP1 = 0, 2048
        F_G8 = 4096
        F_PP8 = 12288

        ST_MU1, ST_RHO, ST_R1, ST_MUP, ST_RHOP = 0, 8, 16, 24, 32
        ST_A = 40
        ST_GN = 56
        ST_SW = 60
        ST_TH = 64
        ST_Q = 68
        ST_ACC = 76
        ST_MUD = 80
        ST_LN = 84

        def stt(i):
            return STT[:, i:i + 1]

        def LOAD(dst, src, sem, reads=(), writes=(), waits=()):
            return P.emit("sync", sem, lambda e, d=dst, s=src: e.dma_start(out=d, in_=s),
                          reads=reads, writes=writes, waits=waits)

        def STORE(dst, src, sem, reads=(), writes=(), waits=()):
            return P.emit("sync", sem, lambda e, d=dst, s=src: e.dma_start(out=d, in_=s),
                          reads=reads, writes=writes, waits=waits)

        def V(fn, reads=(), writes=(), waits=()):
            return P.emit("vector", vE, fn, reads=reads, writes=writes, waits=waits)

        def S(fn, reads=(), writes=(), waits=()):
            return P.emit("scalar", sE, fn, reads=reads, writes=writes, waits=waits)

        def MM(fn, reads=(), writes=(), waits=()):
            return P.emit("tensor", tE, fn, reads=reads, writes=writes, waits=waits)

        rr = [0]

        def bank():
            b = rr[0] % 8
            rr[0] += 1
            return b

        bID, bONE = Buf(), Buf()
        LOAD(IDn[:], IDEN, dC, writes=[bID])
        LOAD(ONE[:], ONE1, dC, writes=[bONE])
        LOAD(OCB[:], OCOLB, dC, writes=[bOC])
        LOAD(ar(A_VDP, 128), VDECP, dC, writes=[reg("VDP")])
        LOAD(ar(A_VDO, 128), VDECO, dC, writes=[reg("VDO")])
        LOAD(ar(A_GCG, 128), GCGD, dC, writes=[reg("GCG")])
        LOAD(NRW[:], N1W, dNR, writes=[bNRM])
        LOAD(NRB[:], N1B, dNR, writes=[bNRM])
        for _b in (bID, bONE, bOC, reg("VDP"), reg("VDO"), reg("GCG")):
            _b.w = (dC, dC.n)
        debt = 0.0
        for i, k in enumerate(range(7, -1, -1)):
            cval = -(float(2 ** k) + debt)
            V(lambda e, c=cval, i=i: e.memset(CST[:, i:i + 1], c), writes=[bCST])
            debt += 2 ** (k - 1)
        V(lambda e: e.memset(OCF[:], 1.0), writes=[bOC])
        V(lambda e: e.memset(ar(A_MST, 2048), 0.0), writes=[reg("MST")])

        def ln_stats(src_ap, n, mu_slot, r_slot, rho_slot=None, dump_off=A_DUMP,
                     src_bufs=(), waits=(), dump_buf=None):
            inv_n = 1.0 / float(n)
            dmp = dump_buf if dump_buf is not None else reg("DUMPX")
            S(lambda e: e.activation(ar(dump_off, n), src_ap, AF.Square,
                                     accum_out=stt(ST_A + 0)),
              reads=list(src_bufs), writes=[dmp, bSTT], waits=list(waits))
            V(lambda e: e.reduce_sum(stt(ST_A + 1), src_ap, AX.X),
              reads=list(src_bufs), writes=[bSTT], waits=list(waits))
            V(lambda e: e.tensor_scalar(stt(mu_slot), stt(ST_A + 1), inv_n, None, ALU.mult),
              reads=[bSTT], writes=[bSTT])
            V(lambda e: e.tensor_scalar(stt(ST_A + 2), stt(ST_A + 0), inv_n, None, ALU.mult),
              reads=[bSTT], writes=[bSTT])
            V(lambda e: e.tensor_tensor(stt(ST_A + 3), stt(mu_slot), stt(mu_slot), ALU.mult),
              reads=[bSTT], writes=[bSTT])
            V(lambda e: e.tensor_tensor(stt(ST_A + 4), stt(ST_A + 2), stt(ST_A + 3), ALU.subtract),
              reads=[bSTT], writes=[bSTT])
            V(lambda e: e.tensor_scalar(stt(ST_A + 5), stt(ST_A + 4), EPS, None, ALU.add),
              reads=[bSTT], writes=[bSTT])
            S(lambda e: e.activation(stt(ST_A + 6), stt(ST_A + 5), AF.Sqrt),
              reads=[bSTT], writes=[bSTT])
            V(lambda e: e.reciprocal(stt(r_slot), stt(ST_A + 6)), reads=[bSTT], writes=[bSTT])
            V(lambda e: e.tensor_tensor(stt(ST_A + 6), stt(r_slot), stt(r_slot), ALU.mult),
              reads=[bSTT], writes=[bSTT])
            V(lambda e: e.tensor_tensor(stt(ST_A + 6), stt(ST_A + 6), stt(ST_A + 5), ALU.mult),
              reads=[bSTT], writes=[bSTT])
            V(lambda e: e.tensor_scalar(stt(ST_A + 6), stt(ST_A + 6), -0.5, 1.5, ALU.mult, ALU.add),
              reads=[bSTT], writes=[bSTT])
            tok = V(lambda e: e.tensor_tensor(stt(r_slot), stt(r_slot), stt(ST_A + 6), ALU.mult),
                    reads=[bSTT], writes=[bSTT])
            if rho_slot is None:
                return tok
            V(lambda e: e.tensor_tensor(stt(ST_A + 6), stt(r_slot), stt(r_slot), ALU.mult),
              reads=[bSTT], writes=[bSTT])
            V(lambda e: e.tensor_tensor(stt(ST_A + 6), stt(ST_A + 6), stt(ST_A + 4), ALU.mult),
              reads=[bSTT], writes=[bSTT])
            V(lambda e: e.tensor_scalar(stt(ST_A + 6), stt(ST_A + 6), EPS, None, ALU.add),
              reads=[bSTT], writes=[bSTT])
            S(lambda e: e.activation(stt(ST_A + 7), stt(ST_A + 6), AF.Sqrt),
              reads=[bSTT], writes=[bSTT])
            V(lambda e: e.reciprocal(stt(ST_A + 7), stt(ST_A + 7)), reads=[bSTT], writes=[bSTT])
            V(lambda e: e.tensor_tensor(stt(ST_A + 8), stt(ST_A + 7), stt(ST_A + 7), ALU.mult),
              reads=[bSTT], writes=[bSTT])
            V(lambda e: e.tensor_tensor(stt(ST_A + 8), stt(ST_A + 8), stt(ST_A + 6), ALU.mult),
              reads=[bSTT], writes=[bSTT])
            V(lambda e: e.tensor_scalar(stt(ST_A + 8), stt(ST_A + 8), -0.5, 1.5, ALU.mult, ALU.add),
              reads=[bSTT], writes=[bSTT])
            V(lambda e: e.tensor_tensor(stt(ST_A + 7), stt(ST_A + 7), stt(ST_A + 8), ALU.mult),
              reads=[bSTT], writes=[bSTT])
            return V(lambda e: e.tensor_tensor(stt(rho_slot), stt(r_slot), stt(ST_A + 7), ALU.mult),
                     reads=[bSTT], writes=[bSTT])

        bXST = reg("XST")

        for cc in range(NCH):
            LOAD(ar(A_XST, D), xo[cc * 128:(cc + 1) * 128, :], dX, writes=[bXST])
            ln_stats(ar(A_XST, D), D, ST_MU1 + cc, ST_R1 + cc, ST_RHO + cc, src_bufs=[bXST])
        for cc in range(NCH):
            LOAD(ar(A_XST, D), xp[cc * 128:(cc + 1) * 128, :], dX, writes=[bXST])
            ln_stats(ar(A_XST, D), D, ST_MUP + cc, ST_A + 9, ST_RHOP + cc, src_bufs=[bXST])

        def transpose128(src_off, dst_off, src_buf, dst_buf, cast_bf16=False, ab=False):
            b = bank()
            MM(lambda e, b=b: e.transpose(ps[b][:, 0:128], ar(src_off, 128), IDn[:]),
               reads=[src_buf, bID], writes=[bPS[b]])
            dst = (AB[:, dst_off:dst_off + 128] if ab else ar(dst_off, 128))
            return V(lambda e, b=b, dst=dst: e.tensor_copy(dst, ps[b][:, 0:128]),
                     reads=[bPS[b]], writes=[dst_buf])

        def make_h(xsrc, mu_slot, rho_slot):
            LOAD(ar(A_XST, D), xsrc, dX, writes=[bXST])
            V(lambda e: e.tensor_scalar(ar(A_XST, D), ar(A_XST, D), stt(mu_slot),
                                        stt(rho_slot), ALU.subtract, ALU.mult),
              reads=[bXST, bSTT], writes=[bXST])
            V(lambda e: e.tensor_tensor(ar(A_XST, D), ar(A_XST, D), NRW[:], ALU.mult),
              reads=[bXST, bNRM], writes=[bXST])
            V(lambda e: e.tensor_tensor(ar(A_XST, D), ar(A_XST, D), NRB[:], ALU.add),
              reads=[bXST, bNRM], writes=[bXST])
            for k in range(16):
                transpose128(A_XST + k * 128, A_HFM + k * 128, bXST, reg("HFM"))

        def proj_tm(Wdram, dst_off, dst_buf, evac_kind, n_oc=4):
            for o in range(n_oc):
                b = bank()
                for k in range(16):
                    wi = (o * 16 + k) % 4
                    LOAD(WBt[wi][:], Wdram[k * 128:(k + 1) * 128, o * 512:(o + 1) * 512],
                         dW[wi], writes=[bWB[wi]])
                    MM(lambda e, b=b, k=k, wi=wi: e.matmul(
                        ps[b][:], AR[:, A_HFM + k * 128:A_HFM + (k + 1) * 128], WBt[wi][:],
                        start=(k == 0), stop=(k == 15)),
                        reads=[bWB[wi], reg("HFM")],
                        writes=([bPS[b]] if k in (0, 15) else []))
                if evac_kind == "copy":
                    V(lambda e, b=b, o=o: e.tensor_copy(ar(dst_off + o * 512, 512), ps[b][:]),
                      reads=[bPS[b]], writes=[dst_buf])
                elif evac_kind == "kscale":
                    V(lambda e, b=b, o=o: e.tensor_scalar(ar(dst_off + o * 512, 512),
                                                          ps[b][:], float(128 ** -0.5),
                                                          None, ALU.mult),
                      reads=[bPS[b]], writes=[dst_buf])
                elif evac_kind == "silu":
                    S(lambda e, b=b, o=o: e.activation(ar(dst_off + o * 512, 512), ps[b][:],
                                                       AF.Silu),
                      reads=[bPS[b]], writes=[dst_buf])

        def attn_chunk(cg_idx, is_prefix):
            if is_prefix:
                make_h(xp[cg_idx * 128:(cg_idx + 1) * 128, :], ST_MUP + cg_idx, ST_RHOP + cg_idx)
            else:
                make_h(xo[cg_idx * 128:(cg_idx + 1) * 128, :], ST_MU1 + cg_idx, ST_RHO + cg_idx)
            proj_tm(WK, A_KTM, reg("KTM"), "kscale")
            proj_tm(WV, A_VTM, reg("VTM"), "copy")
            if not is_prefix:
                proj_tm(WQ, A_QTM, reg("QTM"), "copy")
                proj_tm(WG, A_GG, reg("GG"), "silu")
            vtbl = A_VDP if is_prefix else A_VDO
            vreg = reg("VDP") if is_prefix else reg("VDO")
            for hh in range(NH):
                if not is_prefix:
                    LOAD(ar(A_MSK, 128), MSKD[hh], dK1, writes=[reg("MSK")])
                    LOAD(ar(A_GL, 128), GLD[hh], dK2, writes=[reg("GL")])
                    transpose128(A_QTM + hh * 128, A_QH, reg("QTM"), reg("QH"))
                    transpose128(A_KTM + hh * 128, A_KH, reg("KTM"), reg("KH"))
                    V(lambda e, hh=hh, cg=cg_idx: e.scalar_tensor_tensor(
                        ar(A_QSC, 128), ar(A_QH, 128),
                        AR[:, A_GCG + hh * 8 + cg:A_GCG + hh * 8 + cg + 1],
                        ar(A_GL, 128), ALU.mult, ALU.mult),
                      reads=[reg("QH"), reg("GCG"), reg("GL")], writes=[reg("QSC")])
                    b1 = bank()
                    MM(lambda e, b=b1: e.matmul(ps[b][:, 0:128], ar(A_KH, 128),
                                                ar(A_QH, 128), start=True, stop=True),
                       reads=[reg("KH"), reg("QH")], writes=[bPS[b1]])
                    V(lambda e, b=b1: e.tensor_tensor(ar(A_SCT, 128), ps[b][:, 0:128],
                                                      ar(A_MSK, 128), ALU.mult),
                      reads=[bPS[b1], reg("MSK")], writes=[reg("SCT")])
                    b2 = bank()
                    MM(lambda e, b=b2, hh=hh: e.matmul(
                        ps[b][:, 0:128], AR[:, A_VTM + hh * 128:A_VTM + (hh + 1) * 128],
                        ar(A_SCT, 128), start=True, stop=False),
                       reads=[reg("VTM"), reg("SCT")], writes=[bPS[b2]])
                    MM(lambda e, b=b2, hh=hh: e.matmul(
                        ps[b][:, 0:128], AR[:, A_MST + hh * 128:A_MST + (hh + 1) * 128],
                        ar(A_QSC, 128), start=False, stop=True),
                       reads=[reg("MST"), reg("QSC")], writes=[bPS[b2]])
                    V(lambda e, b=b2: e.tensor_copy(ar(A_RFM, 128), ps[b][:, 0:128]),
                      reads=[bPS[b2]], writes=[reg("RFM")],
                      waits=[(tE, tE.n)])
                    transpose128(A_RFM, A_RET + hh * 128, reg("RFM"), reg("RET"))
                V(lambda e, hh=hh, cg=cg_idx, vt=vtbl: e.tensor_scalar(
                    ar(A_TMP, 128), AR[:, A_VTM + hh * 128:A_VTM + (hh + 1) * 128],
                    AR[:, vt + hh * 8 + cg:vt + hh * 8 + cg + 1], None, ALU.mult),
                  reads=[reg("VTM"), vreg], writes=[reg("TMP")])
                b3 = bank()
                MM(lambda e, b=b3, hh=hh: e.matmul(
                    ps[b][:, 0:128], AR[:, A_KTM + hh * 128:A_KTM + (hh + 1) * 128],
                    ar(A_TMP, 128), start=True, stop=True),
                   reads=[reg("KTM"), reg("TMP")], writes=[bPS[b3]])
                V(lambda e, b=b3, hh=hh: e.tensor_tensor(
                    AR[:, A_MST + hh * 128:A_MST + (hh + 1) * 128], ps[b][:, 0:128],
                    AR[:, A_MST + hh * 128:A_MST + (hh + 1) * 128], ALU.add),
                  reads=[bPS[b3]], writes=[reg("MST")])

        for cc in range(NCH):
            attn_chunk(cc, True)

        for cc in range(NCH):
            attn_chunk(cc, False)
            for hh in range(NH):
                roff = A_RET + hh * 128
                S(lambda e, roff=roff: e.activation(ar(A_DUMP, 128), ar(roff, 128),
                                                    AF.Square, accum_out=stt(ST_A + 0)),
                  reads=[reg("RET")], writes=[reg("DUMPX"), bSTT])
                V(lambda e, roff=roff: e.reduce_sum(stt(ST_A + 1), ar(roff, 128), AX.X),
                  reads=[reg("RET")], writes=[bSTT])
                V(lambda e: e.tensor_scalar(stt(ST_GN), stt(ST_A + 1), 1.0 / 128, None, ALU.mult),
                  reads=[bSTT], writes=[bSTT])
                V(lambda e: e.tensor_scalar(stt(ST_A + 2), stt(ST_A + 0), 1.0 / 128, None, ALU.mult),
                  reads=[bSTT], writes=[bSTT])
                V(lambda e: e.tensor_tensor(stt(ST_A + 3), stt(ST_GN), stt(ST_GN), ALU.mult),
                  reads=[bSTT], writes=[bSTT])
                V(lambda e: e.tensor_tensor(stt(ST_A + 4), stt(ST_A + 2), stt(ST_A + 3), ALU.subtract),
                  reads=[bSTT], writes=[bSTT])
                V(lambda e: e.tensor_scalar(stt(ST_A + 5), stt(ST_A + 4), EPS, None, ALU.add),
                  reads=[bSTT], writes=[bSTT])
                S(lambda e: e.activation(stt(ST_A + 6), stt(ST_A + 5), AF.Sqrt),
                  reads=[bSTT], writes=[bSTT])
                V(lambda e: e.reciprocal(stt(ST_GN + 1), stt(ST_A + 6)), reads=[bSTT], writes=[bSTT])
                V(lambda e: e.tensor_tensor(stt(ST_A + 6), stt(ST_GN + 1), stt(ST_GN + 1), ALU.mult),
                  reads=[bSTT], writes=[bSTT])
                V(lambda e: e.tensor_tensor(stt(ST_A + 6), stt(ST_A + 6), stt(ST_A + 5), ALU.mult),
                  reads=[bSTT], writes=[bSTT])
                V(lambda e: e.tensor_scalar(stt(ST_A + 6), stt(ST_A + 6), -0.5, 1.5, ALU.mult, ALU.add),
                  reads=[bSTT], writes=[bSTT])
                V(lambda e: e.tensor_tensor(stt(ST_GN + 1), stt(ST_GN + 1), stt(ST_A + 6), ALU.mult),
                  reads=[bSTT], writes=[bSTT])
                V(lambda e, roff=roff: e.tensor_scalar(ar(roff, 128), ar(roff, 128),
                                                       stt(ST_GN), stt(ST_GN + 1),
                                                       ALU.subtract, ALU.mult),
                  reads=[reg("RET"), bSTT], writes=[reg("RET")])
                V(lambda e, hh=hh: e.tensor_tensor(
                    AR[:, A_GG + hh * 128:A_GG + (hh + 1) * 128],
                    AR[:, A_GG + hh * 128:A_GG + (hh + 1) * 128],
                    AR[:, A_RET + hh * 128:A_RET + (hh + 1) * 128], ALU.mult),
                  reads=[reg("GG"), reg("RET")], writes=[reg("GG")])
            for k in range(16):
                transpose128(A_GG + k * 128, A_RET + k * 128, reg("GG"), reg("RET"))
            LOAD(ar(A_XST, D), xo[cc * 128:(cc + 1) * 128, :], dX, writes=[bXST])
            for o in range(4):
                b = bank()
                for k in range(16):
                    wi = (o * 16 + k) % 4
                    LOAD(WBt[wi][:], WO[k * 128:(k + 1) * 128, o * 512:(o + 1) * 512],
                         dW[wi], writes=[bWB[wi]])
                    MM(lambda e, b=b, k=k, wi=wi: e.matmul(
                        ps[b][:], AR[:, A_RET + k * 128:A_RET + (k + 1) * 128], WBt[wi][:],
                        start=(k == 0), stop=(k == 15)),
                        reads=[bWB[wi], reg("RET")],
                        writes=([bPS[b]] if k in (0, 15) else []))
                V(lambda e, o=o, cc=cc: e.tensor_scalar(
                    ar(A_TMP, 512), AR[:, A_XST + o * 512:A_XST + (o + 1) * 512],
                    stt(ST_MU1 + cc), stt(ST_R1 + cc), ALU.subtract, ALU.mult),
                  reads=[bXST, bSTT], writes=[reg("TMP")])
                V(lambda e, b=b, o=o, cc=cc: e.tensor_tensor(
                    Z[:, cc * D + o * 512:cc * D + (o + 1) * 512], ar(A_TMP, 512),
                    ps[b][:], ALU.add),
                  reads=[reg("TMP"), bPS[b]], writes=[bZ[cc]])

        if dbg == "z":
            for cc in range(NCH):
                STORE(Y[cc * 128:(cc + 1) * 128, :], Z[:, cc * D:(cc + 1) * D],
                      dS, reads=[bZ[cc]])
            P.barrier("sync", [(dS, dS.n)])
            _finish(block, P)
            return nc

        def absmean(Wd, nK, nO, acc_slot):
            V(lambda e: e.memset(stt(acc_slot), 0.0), writes=[bSTT])
            for k in range(nK):
                for o in range(nO):
                    wi = (k * nO + o) % 4
                    LOAD(WBt[wi][:], Wd[k * 128:(k + 1) * 128, o * 512:(o + 1) * 512],
                         dW[wi], writes=[bWB[wi]])
                    S(lambda e, wi=wi: e.activation(ar(A_DUMP, 512), WBt[wi][:], AF.Abs,
                                                    accum_out=stt(ST_A + 0)),
                      reads=[bWB[wi]], writes=[reg("DUMPX"), bSTT])
                    V(lambda e: e.tensor_tensor(stt(acc_slot), stt(acc_slot),
                                                stt(ST_A + 0), ALU.add),
                      reads=[bSTT], writes=[bSTT])

        absmean(WGT, 16, 16, ST_ACC + 0)
        absmean(WUT, 16, 16, ST_ACC + 1)
        absmean(WDT, 64, 4, ST_ACC + 2)

        def reduce_rep(acc_slot, out_slot, scale):
            b = bank()
            MM(lambda e, b=b: e.matmul(ps[b][0:1, 0:1], stt(acc_slot), OCF[:],
                                       start=True, stop=True),
               reads=[bSTT, bOC], writes=[bPS[b]])
            V(lambda e, b=b: e.tensor_copy(NEG[0:1, 0:1], ps[b][0:1, 0:1]),
              reads=[bPS[b]], writes=[bNEG])
            b2 = bank()
            MM(lambda e, b=b2: e.matmul(ps[b][:, 0:1], ONE[:], NEG[0:1, 0:1],
                                        start=True, stop=True),
               reads=[bNEG, bONE], writes=[bPS[b2]])
            V(lambda e, b=b2: e.tensor_scalar(stt(out_slot), ps[b][:, 0:1], scale,
                                              1e-5, ALU.mult, ALU.max),
              reads=[bPS[b2]], writes=[bSTT])

        reduce_rep(ST_ACC + 0, ST_SW + 0, 2.0 ** -24)
        reduce_rep(ST_ACC + 1, ST_SW + 1, 2.0 ** -24)
        reduce_rep(ST_ACC + 2, ST_SW + 2, 2.0 ** -24)
        if dbg == "sw":
            STORE(Y[0:128, 0:128], STT[:, :], dS, reads=[bSTT])

        def ternarize(Wd, Md, NCd, nK, nO, sw_slot):
            V(lambda e: e.tensor_scalar(stt(ST_TH), stt(sw_slot), -0.5, None, ALU.mult),
              reads=[bSTT], writes=[bSTT])
            V(lambda e: e.tensor_scalar(stt(ST_TH + 1), stt(sw_slot), 0.5, None, ALU.mult),
              reads=[bSTT], writes=[bSTT])
            for o in range(nO):
                bcs = bank()
                for k in range(nK):
                    wi = k % 4
                    LOAD(WBt[wi][:], Wd[k * 128:(k + 1) * 128, o * 512:(o + 1) * 512],
                         dW[wi], writes=[bWB[wi]])
                    S(lambda e, wi=wi: e.activation(ar(A_DUMP, 512), WBt[wi][:], AF.Sign,
                                                    bias=stt(ST_TH + 1)),
                      reads=[bWB[wi], bSTT], writes=[reg("SGA")])
                    S(lambda e, wi=wi: e.activation(ar(A_DUMP + 512, 512), WBt[wi][:], AF.Sign,
                                                    bias=stt(ST_TH)),
                      reads=[bWB[wi], bSTT], writes=[reg("SGB")])
                    mi = k % 2
                    V(lambda e, mi=mi: e.tensor_tensor(MBt[mi][:], ar(A_DUMP, 512),
                                                       ar(A_DUMP + 512, 512), ALU.add),
                      reads=[reg("SGA"), reg("SGB")], writes=[bMB[mi]])
                    MM(lambda e, b=bcs, mi=mi, k=k, nK=nK: e.matmul(
                        ps[b][0:1, :], OCB[:], MBt[mi][:],
                        start=(k == 0), stop=(k == nK - 1)),
                       reads=[bMB[mi], bOC],
                       writes=([bPS[bcs]] if k in (0, nK - 1) else []))
                    STORE(Md[k * 128:(k + 1) * 128, o * 512:(o + 1) * 512], MBt[mi][:],
                          dSM[mi], reads=[bMB[mi]], waits=[(tE, tE.n)])
                V(lambda e, b=bcs: e.tensor_scalar(NEG[0:1, 0:512], ps[b][0:1, :],
                                                   -1.0, None, ALU.mult),
                  reads=[bPS[bcs]], writes=[bNEG])
                STORE(NCd[0:1, o * 512:(o + 1) * 512], NEG[0:1, 0:512], dSN, reads=[bNEG])

        ternarize(WGT, M2G, NCG, 16, 16, ST_SW + 0)
        ternarize(WUT, M2U, NCU, 16, 16, ST_SW + 1)
        ternarize(WDT, M2D, NCD, 64, 4, ST_SW + 2)
        tern_done = [(dSM[0], dSM[0].n), (dSM[1], dSM[1].n), (dSN, dSN.n)]

        LOAD(NRW[:], N2W, dNR, writes=[bNRM])
        LOAD(NRB[:], N2B, dNR, writes=[bNRM])
        bH2 = reg("H2FM")
        for gg in range(2):
            tchunks = list(range(gg * 4, (gg + 1) * 4))
            for ci, cc in enumerate(tchunks):
                zsl = Z[:, cc * D:(cc + 1) * D]
                ln_stats(zsl, D, ST_LN, ST_LN + 1, None, dump_off=B_H2ST,
                         src_bufs=[bZ[cc]], dump_buf=reg("H2ST"))
                V(lambda e, zsl=zsl: e.tensor_scalar(ar(B_H2ST, D), zsl, stt(ST_LN),
                                                     stt(ST_LN + 1), ALU.subtract, ALU.mult),
                  reads=[bZ[cc], bSTT], writes=[reg("H2ST")])
                V(lambda e: e.tensor_tensor(ar(B_H2ST, D), ar(B_H2ST, D), NRW[:], ALU.mult),
                  reads=[reg("H2ST"), bNRM], writes=[reg("H2ST")])
                V(lambda e: e.tensor_tensor(ar(B_H2ST, D), ar(B_H2ST, D), NRB[:], ALU.add),
                  reads=[reg("H2ST"), bNRM], writes=[reg("H2ST")])
                for k in range(16):
                    transpose128(B_H2ST + k * 128, B_H2FM + ci * D + k * 128,
                                 reg("H2ST"), bH2)
            for oi in range(11):
                for ci in range(4):
                    b1 = bank()
                    for k in range(16):
                        wi = k % 4
                        LOAD(WBt[wi][:], W1[k * 128:(k + 1) * 128, oi * 512:(oi + 1) * 512],
                             dW[wi], writes=[bWB[wi]])
                        MM(lambda e, b=b1, k=k, wi=wi, ci=ci: e.matmul(
                            ps[b][:], AR[:, B_H2FM + ci * D + k * 128:B_H2FM + ci * D + (k + 1) * 128],
                            WBt[wi][:], start=(k == 0), stop=(k == 15)),
                            reads=[bWB[wi], bH2],
                            writes=([bPS[b1]] if k in (0, 15) else []))
                    S(lambda e, b=b1: e.activation(ar(B_UST, 512), ps[b][:], AF.Silu),
                      reads=[bPS[b1]], writes=[reg("UST")])
                    b2 = bank()
                    for k in range(16):
                        wi = k % 4
                        LOAD(WBt[wi][:], W2[k * 128:(k + 1) * 128, oi * 512:(oi + 1) * 512],
                             dW[wi], writes=[bWB[wi]])
                        MM(lambda e, b=b2, k=k, wi=wi, ci=ci: e.matmul(
                            ps[b][:], AR[:, B_H2FM + ci * D + k * 128:B_H2FM + ci * D + (k + 1) * 128],
                            WBt[wi][:], start=(k == 0), stop=(k == 15)),
                            reads=[bWB[wi], bH2],
                            writes=([bPS[b2]] if k in (0, 15) else []))
                    V(lambda e, b=b2: e.tensor_tensor(ar(B_PTM, 512), ps[b][:],
                                                      ar(B_UST, 512), ALU.mult),
                      reads=[bPS[b2], reg("UST")], writes=[reg("PTM")])
                    for k2 in range(4):
                        transpose128(B_PTM + k2 * 128, B_PFM + k2 * 128, reg("PTM"), reg("PFM"))
                    for oo in range(4):
                        b4 = bank()
                        for k2 in range(4):
                            wi = k2 % 4
                            kk = oi * 4 + k2
                            LOAD(WBt[wi][:], W3[kk * 128:(kk + 1) * 128, oo * 512:(oo + 1) * 512],
                                 dW[wi], writes=[bWB[wi]])
                            MM(lambda e, b=b4, k2=k2, wi=wi: e.matmul(
                                ps[b][:], AR[:, B_PFM + k2 * 128:B_PFM + (k2 + 1) * 128],
                                WBt[wi][:], start=(k2 == 0), stop=(k2 == 3)),
                                reads=[bWB[wi], reg("PFM")],
                                writes=([bPS[b4]] if k2 in (0, 3) else []))
                        yoff = B_Y2 + ci * D + oo * 512
                        if oi == 0:
                            V(lambda e, b=b4, yoff=yoff: e.tensor_copy(ar(yoff, 512), ps[b][:]),
                              reads=[bPS[b4]], writes=[reg(f"Y2_{ci}_{oo}")])
                        else:
                            V(lambda e, b=b4, yoff=yoff: e.tensor_tensor(
                                ar(yoff, 512), ps[b][:], ar(yoff, 512), ALU.add),
                              reads=[bPS[b4]], writes=[reg(f"Y2_{ci}_{oo}")])
            for ci, cc in enumerate(tchunks):
                zsl = Z[:, cc * D:(cc + 1) * D]
                y2bufs = [reg(f"Y2_{ci}_{oo}") for oo in range(4)]
                V(lambda e, zsl=zsl, ci=ci: e.tensor_tensor(
                    zsl, AR[:, B_Y2 + ci * D:B_Y2 + (ci + 1) * D], zsl, ALU.add),
                  reads=y2bufs + [bZ[cc]], writes=[bZ[cc]])
                ln_stats(zsl, D, ST_LN, ST_LN + 1, None, dump_off=B_H2ST,
                         src_bufs=[bZ[cc]], dump_buf=reg("H2ST"))
                LOAD(ar(B_H2ST, D), xo[cc * 128:(cc + 1) * 128, :], dH2, writes=[reg("H2ST")])
                V(lambda e, zsl=zsl: e.tensor_scalar(zsl, zsl, stt(ST_LN),
                                                     stt(ST_LN + 1), ALU.subtract, ALU.mult),
                  reads=[bZ[cc], bSTT], writes=[bZ[cc]])
                V(lambda e, zsl=zsl: e.tensor_tensor(zsl, ar(B_H2ST, D), zsl, ALU.add),
                  reads=[reg("H2ST"), bZ[cc]], writes=[bZ[cc]])

        if dbg == "out1":
            for cc in range(NCH):
                STORE(Y[cc * 128:(cc + 1) * 128, :], Z[:, cc * D:(cc + 1) * D],
                      dS, reads=[bZ[cc]])
            P.barrier("sync", [(dS, dS.n)])
            _finish(block, P)
            return nc

        P.barrier("sync", tern_done)
        if dbg == "m2g0":
            for kk2 in range(16):
                LOAD(MBt[0][:, 0:128], M2G[kk2 * 128:(kk2 + 1) * 128, 0:128], dM[0],
                     writes=[bMB[0]])
                V(lambda e, kk2=kk2: e.tensor_copy(ar(F_G8 + kk2 * 128, 128), MBt[0][:, 0:128]),
                  reads=[bMB[0]], writes=[reg("G8")])
            STORE(Y[0:128, 0:2048], ar(F_G8, D), dS, reads=[reg("G8")])
            LOAD(NEG[0:1, 0:512], NCG[0:1, 0:512], dN, writes=[bNEG])
            V(lambda e: e.tensor_copy(ar(F_PP0, 512)[0:1, :], NEG[0:1, 0:512]),
              reads=[bNEG], writes=[reg("PP0")])
            STORE(Y[128:129, 0:512], ar(F_PP0, 512)[0:1, :], dS, reads=[reg("PP0")])
        for cc in range(NCH):
            zsl = Z[:, cc * D:(cc + 1) * D]
            ln_stats(zsl, D, ST_LN, ST_LN + 1, None, dump_off=F_G8, src_bufs=[bZ[cc]],
                     dump_buf=reg("G8"))
            V(lambda e, zsl=zsl: e.tensor_scalar(ar(F_PP0, D), zsl, stt(ST_LN),
                                                 stt(ST_LN + 1), ALU.subtract, ALU.mult),
              reads=[bZ[cc], bSTT], writes=[reg("PP0")])
            cur, oth = F_PP0, F_PP1
            rcur, roth = reg("PP0"), reg("PP1")
            hstep = 1
            while hstep < D:
                def mka(e, cur=cur, oth=oth, hstep=hstep):
                    ia = AR[:, cur:cur + D].rearrange("p (b two h) -> p b two h", two=2, h=hstep)
                    oa = AR[:, oth:oth + D].rearrange("p (b two h) -> p b two h", two=2, h=hstep)
                    return e.tensor_tensor(oa[:, :, 0, :], ia[:, :, 0, :], ia[:, :, 1, :], ALU.add)

                def mkb(e, cur=cur, oth=oth, hstep=hstep):
                    ia = AR[:, cur:cur + D].rearrange("p (b two h) -> p b two h", two=2, h=hstep)
                    oa = AR[:, oth:oth + D].rearrange("p (b two h) -> p b two h", two=2, h=hstep)
                    return e.tensor_tensor(oa[:, :, 1, :], ia[:, :, 0, :], ia[:, :, 1, :], ALU.subtract)
                V(mka, reads=[rcur], writes=[])
                V(mkb, reads=[rcur], writes=[roth])
                cur, oth = oth, cur
                rcur, roth = roth, rcur
                hstep *= 2
            z1o, z1b = cur, rcur
            o1o, o1b = oth, roth
            V(lambda e, z1o=z1o: e.reduce_max(stt(ST_Q), ar(z1o, D), AX.X,
                                              apply_absolute_value=True),
              reads=[z1b], writes=[bSTT])
            V(lambda e: e.tensor_scalar(stt(ST_Q + 1), stt(ST_Q), C2048, 1e-5,
                                        ALU.mult, ALU.max), reads=[bSTT], writes=[bSTT])
            V(lambda e: e.reciprocal(stt(ST_Q + 2), stt(ST_Q + 1)), reads=[bSTT], writes=[bSTT])
            V(lambda e: e.tensor_scalar(stt(ST_Q + 3), stt(ST_Q + 2), 127.0 * C2048,
                                        None, ALU.mult), reads=[bSTT], writes=[bSTT])
            V(lambda e: e.tensor_scalar(stt(ST_Q + 4), stt(ST_Q + 1), 1.0 / 254.0,
                                        None, ALU.mult), reads=[bSTT], writes=[bSTT])
            V(lambda e: e.tensor_tensor(stt(ST_Q + 5), stt(ST_Q + 4), stt(ST_SW + 0), ALU.mult),
              reads=[bSTT], writes=[bSTT])
            V(lambda e: e.tensor_tensor(stt(ST_Q + 6), stt(ST_Q + 4), stt(ST_SW + 1), ALU.mult),
              reads=[bSTT], writes=[bSTT])

            def cascade(zoff, zbuf, n, bA, bAb, bB, bBb):
                V(lambda e: e.tensor_scalar(ar(bB, n), ar(zoff, n), stt(ST_Q + 3),
                                            128.5, ALU.mult, ALU.add),
                  reads=[zbuf, bSTT], writes=[bBb])
                V(lambda e: e.tensor_copy(ar(bA, n), ar(bB, n)), reads=[bBb], writes=[bAb])
                for i, k in enumerate(range(7, -1, -1)):
                    S(lambda e, i=i, n=n: e.activation(ar(zoff, n), ar(bA, n), AF.Sign,
                                                       bias=CST[:, i:i + 1]),
                      reads=[bAb, bCST], writes=[zbuf])
                    V(lambda e, k=k, n=n: e.scalar_tensor_tensor(
                        ar(bA, n), ar(zoff, n), -float(2 ** (k - 1)), ar(bA, n),
                        ALU.mult, ALU.add), reads=[zbuf, bAb], writes=[bAb])
                V(lambda e, n=n: e.scalar_tensor_tensor(
                    ar(zoff, n), ar(bB, n), 0.5, ar(bA, n), ALU.add, ALU.subtract),
                  reads=[bBb, bAb], writes=[zbuf])

            cascade(z1o, z1b, D, o1o, o1b, F_G8, reg("G8"))
            if dbg == "n1op":
                STORE(Y[cc * 128:(cc + 1) * 128, :], ar(z1o, D), dS, reads=[z1b])
            for k in range(16):
                transpose128(z1o + k * 128, k * 128, z1b, bAB1, ab=True)
            for o in range(16):
                bg = bank()
                LOAD(NEG[0:1, 0:512], NCG[0:1, o * 512:(o + 1) * 512], dN, writes=[bNEG])
                for k in range(16):
                    mi = k % 2
                    LOAD(MBt[mi][:], M2G[k * 128:(k + 1) * 128, o * 512:(o + 1) * 512],
                         dM[mi], writes=[bMB[mi]])
                    MM(lambda e, b=bg, k=k, mi=mi: e.matmul(
                        ps[b][:], AB[:, k * 128:(k + 1) * 128], MBt[mi][:],
                        start=(k == 0), stop=False),
                        reads=[bMB[mi], bAB1],
                        writes=([bPS[bg]] if k == 0 else []))
                MM(lambda e, b=bg: e.matmul(ps[b][:], ONE[:], NEG[0:1, 0:512],
                                            start=False, stop=True),
                   reads=[bNEG, bONE], writes=[bPS[bg]])
                S(lambda e, b=bg: e.activation(ar(B_UST, 512), ps[b][:], AF.Silu,
                                               scale=stt(ST_Q + 5)),
                  reads=[bPS[bg], bSTT], writes=[reg("UST")], waits=[(tE, tE.n)])
                bu = bank()
                LOAD(NEG[0:1, 512:1024], NCU[0:1, o * 512:(o + 1) * 512], dN, writes=[bNEG])
                for k in range(16):
                    mi = k % 2
                    LOAD(MBt[mi][:], M2U[k * 128:(k + 1) * 128, o * 512:(o + 1) * 512],
                         dM[mi], writes=[bMB[mi]])
                    MM(lambda e, b=bu, k=k, mi=mi: e.matmul(
                        ps[b][:], AB[:, k * 128:(k + 1) * 128], MBt[mi][:],
                        start=(k == 0), stop=False),
                        reads=[bMB[mi], bAB1],
                        writes=([bPS[bu]] if k == 0 else []))
                MM(lambda e, b=bu: e.matmul(ps[b][:], ONE[:], NEG[0:1, 512:1024],
                                            start=False, stop=True),
                   reads=[bNEG, bONE], writes=[bPS[bu]])
                V(lambda e, b=bu, o=o: e.scalar_tensor_tensor(
                    AR[:, F_G8 + o * 512:F_G8 + (o + 1) * 512], ps[b][:], stt(ST_Q + 6),
                    ar(B_UST, 512), ALU.mult, ALU.mult),
                  reads=[bPS[bu], reg("UST"), bSTT], writes=[reg("G8")],
                  waits=[(tE, tE.n)])
            if dbg == "g8a":
                STORE(Y[cc * 128:(cc + 1) * 128, :], ar(F_G8, D), dS, reads=[reg("G8")])
            cur8, oth8 = F_G8, F_PP8
            rc8, ro8 = reg("G8"), reg("PP8")
            hstep = 1
            while hstep < FI:
                def mka(e, cur=cur8, oth=oth8, hstep=hstep):
                    ia = AR[:, cur:cur + FI].rearrange("p (b two h) -> p b two h", two=2, h=hstep)
                    oa = AR[:, oth:oth + FI].rearrange("p (b two h) -> p b two h", two=2, h=hstep)
                    return e.tensor_tensor(oa[:, :, 0, :], ia[:, :, 0, :], ia[:, :, 1, :], ALU.add)

                def mkb(e, cur=cur8, oth=oth8, hstep=hstep):
                    ia = AR[:, cur:cur + FI].rearrange("p (b two h) -> p b two h", two=2, h=hstep)
                    oa = AR[:, oth:oth + FI].rearrange("p (b two h) -> p b two h", two=2, h=hstep)
                    return e.tensor_tensor(oa[:, :, 1, :], ia[:, :, 0, :], ia[:, :, 1, :], ALU.subtract)
                V(mka, reads=[rc8], writes=[])
                V(mkb, reads=[rc8], writes=[ro8])
                cur8, oth8 = oth8, cur8
                rc8, ro8 = ro8, rc8
                hstep *= 2
            z8o, z8b = cur8, rc8
            o8o, o8b = oth8, ro8
            V(lambda e, z8o=z8o: e.reduce_max(stt(ST_Q), ar(z8o, FI), AX.X,
                                              apply_absolute_value=True),
              reads=[z8b], writes=[bSTT])
            V(lambda e: e.tensor_scalar(stt(ST_Q + 1), stt(ST_Q), C8192, 1e-5,
                                        ALU.mult, ALU.max), reads=[bSTT], writes=[bSTT])
            V(lambda e: e.reciprocal(stt(ST_Q + 2), stt(ST_Q + 1)), reads=[bSTT], writes=[bSTT])
            V(lambda e: e.tensor_scalar(stt(ST_Q + 3), stt(ST_Q + 2), 127.0 * C8192,
                                        None, ALU.mult), reads=[bSTT], writes=[bSTT])
            V(lambda e: e.tensor_scalar(stt(ST_Q + 4), stt(ST_Q + 1), 1.0 / 254.0,
                                        None, ALU.mult), reads=[bSTT], writes=[bSTT])
            V(lambda e: e.tensor_tensor(stt(ST_Q + 7), stt(ST_Q + 4), stt(ST_SW + 2), ALU.mult),
              reads=[bSTT], writes=[bSTT])
            cascade(z8o, z8b, 4096, o8o, o8b, o8o + 4096, o8b)
            cascade(z8o + 4096, z8b, 4096, o8o, o8b, o8o + 4096, o8b)
            if dbg == "z2q":
                STORE(Y[cc * 128:(cc + 1) * 128, :], ar(z8o, D), dS, reads=[z8b])
            for k in range(64):
                transpose128(z8o + k * 128, 2048 + k * 128, z8b, bAB8, ab=True)
            dts = []
            dbanks = []
            for o in range(4):
                bd = bank()
                LOAD(NEG[0:1, 0:512], NCD[0:1, o * 512:(o + 1) * 512], dN, writes=[bNEG])
                for k in range(64):
                    mi = k % 2
                    LOAD(MBt[mi][:], M2D[k * 128:(k + 1) * 128, o * 512:(o + 1) * 512],
                         dM[mi], writes=[bMB[mi]])
                    MM(lambda e, b=bd, k=k, mi=mi: e.matmul(
                        ps[b][:], AB[:, 2048 + k * 128:2048 + (k + 1) * 128], MBt[mi][:],
                        start=(k == 0), stop=False),
                        reads=[bMB[mi], bAB8],
                        writes=([bPS[bd]] if k == 0 else []))
                MM(lambda e, b=bd: e.matmul(ps[b][:], ONE[:], NEG[0:1, 0:512],
                                            start=False, stop=True),
                   reads=[bNEG, bONE], writes=[bPS[bd]])
                dbanks.append((bd, (tE, tE.n)))
            for o, (bd, tmtok) in enumerate(dbanks):
                V(lambda e, b=bd, o=o: e.tensor_scalar(
                    AR[:, o8o + o * 512:o8o + (o + 1) * 512], ps[b][:], stt(ST_Q + 7),
                    None, ALU.mult),
                  reads=[bPS[bd], bSTT], writes=[o8b], waits=[tmtok])
            if dbg == "dpre":
                STORE(Y[cc * 128:(cc + 1) * 128, :], ar(o8o, D), dS, reads=[o8b])
            ln_stats(ar(o8o, D), D, ST_MUD, ST_MUD + 1, None, dump_off=z8o,
                     src_bufs=[o8b], dump_buf=z8b)
            V(lambda e, o8o=o8o: e.tensor_scalar(ar(o8o, D), ar(o8o, D), stt(ST_MUD),
                                                 stt(ST_MUD + 1), ALU.subtract, ALU.mult),
              reads=[o8b, bSTT], writes=[o8b])
            V(lambda e, o8o=o8o, cc=cc: e.tensor_tensor(
                ar(o8o, D), ar(o8o, D), Z[:, cc * D:(cc + 1) * D], ALU.add),
              reads=[o8b, bZ[cc]], writes=[o8b])
            if dbg is None:
                STORE(Y[cc * 128:(cc + 1) * 128, :], ar(o8o, D), dS, reads=[o8b])

        P.barrier("sync", [(dS, dS.n)])
        _finish(block, P)
    return nc


def _finish(block, P):
    @block.sync
    def _(sync):
        P.run_engine("sync", sync)

    @block.tensor
    def _(t):
        P.run_engine("tensor", t)

    @block.vector
    def _(v):
        P.run_engine("vector", v)

    @block.scalar
    def _(s):
        P.run_engine("scalar", s)


def _host_tables(j):
    j0 = 8 * j
    maskT = np.zeros((NH, C, C), np.float64)
    glh = np.zeros((NH, C, C), np.float64)
    for h in range(NH):
        g = np.float64(GAMMAS[h])
        lv = np.arange(C, dtype=np.float64)
        diff = lv[None, :] - lv[:, None]
        maskT[h] = np.where(diff >= 0, g ** np.maximum(diff, 0), 0.0)
        glh[h] = np.broadcast_to(g ** lv, (C, C))
    vdecp = np.zeros((128, 128), np.float64)
    vdeco = np.zeros((128, 128), np.float64)
    gcg = np.zeros((128, 128), np.float64)
    for h in range(NH):
        g = np.float64(GAMMAS[h])
        lv = np.arange(C, dtype=np.float64)
        for cch in range(8):
            if j == 1:
                vdecp[:, h * 8 + cch] = g ** (-(C * cch + lv))
            cg = j0 + cch
            vdeco[:, h * 8 + cch] = g ** (-(C * cg + lv))
            gcg[:, h * 8 + cch] = g ** (C * cg)
    return (maskT.astype(np.float32), glh.astype(np.float32),
            vdecp.astype(np.float32), vdeco.astype(np.float32),
            gcg.astype(np.float32))


def kernel(L0_attn_x, L0_attn_norm1_w, L0_attn_norm1_b, L0_attn_wq, L0_attn_wk,
           L0_attn_wv, L0_attn_wg, L0_attn_wo, L0_attn_norm2_w, L0_attn_norm2_b,
           L0_attn_w1, L0_attn_w2, L0_attn_w3, L0_ffn_w_gate, L0_ffn_w_up,
           L0_ffn_w_down):
    f32 = np.float32
    x = np.ascontiguousarray(L0_attn_x, dtype=f32)
    wq = np.ascontiguousarray(L0_attn_wq, f32)
    wk = np.ascontiguousarray(L0_attn_wk, f32)
    wv = np.ascontiguousarray(L0_attn_wv, f32)
    wg = np.ascontiguousarray(L0_attn_wg, f32)
    wo = np.ascontiguousarray(L0_attn_wo, f32)
    w1p = np.zeros((D, HIDP), f32); w1p[:, :HID] = np.asarray(L0_attn_w1, f32)
    w2p = np.zeros((D, HIDP), f32); w2p[:, :HID] = np.asarray(L0_attn_w2, f32)
    w3p = np.zeros((HIDP, D), f32); w3p[:HID, :] = np.asarray(L0_attn_w3, f32)
    wgT = np.ascontiguousarray(np.asarray(L0_ffn_w_gate, f32).T)
    wuT = np.ascontiguousarray(np.asarray(L0_ffn_w_up, f32).T)
    wdT = np.ascontiguousarray(np.asarray(L0_ffn_w_down, f32).T)
    n1w = np.ascontiguousarray(np.broadcast_to(np.asarray(L0_attn_norm1_w, f32), (128, D)))
    n1b = np.ascontiguousarray(np.broadcast_to(np.asarray(L0_attn_norm1_b, f32), (128, D)))
    n2w = np.ascontiguousarray(np.broadcast_to(np.asarray(L0_attn_norm2_w, f32), (128, D)))
    n2b = np.ascontiguousarray(np.broadcast_to(np.asarray(L0_attn_norm2_b, f32), (128, D)))
    iden = np.eye(128, dtype=f32)
    one1 = np.ones((1, 128), f32)
    ocolb = np.ones((128, 1), ml_dtypes.bfloat16)

    import os
    dbg = os.environ.get("KDBG") or None
    key = "nc" + str(dbg)
    if key not in _CACHE:
        _CACHE[key] = build_nc(dbg)
    nc = _CACHE[key]

    tbl = {j: _host_tables(j) for j in (0, 1)}
    in_maps = []
    for c in range(8):
        b, j = c // 2, c % 2
        maskT, glh, vdecp, vdeco, gcg = tbl[j]
        in_maps.append(dict(
            xo=np.ascontiguousarray(x[b, j * T:(j + 1) * T]),
            xp=np.ascontiguousarray(x[b, 0:T]),
            wq=wq, wk=wk, wv=wv, wg=wg, wo_=wo,
            w1p=w1p, w2p=w2p, w3p=w3p, wgT=wgT, wuT=wuT, wdT=wdT,
            n1w=n1w, n1b=n1b, n2w=n2w, n2b=n2b,
            maskT=maskT, glh=glh, vdecp=vdecp, vdeco=vdeco, gcg=gcg,
            iden=iden, one1=one1, ocolb=ocolb,
        ))
    res = run_bass_kernel_spmd(nc, in_maps, core_ids=list(range(8)))
    out = np.empty((4, 2048, D), f32)
    for c in range(8):
        b, j = c // 2, c % 2
        out[b, j * T:(j + 1) * T] = res.results[c]["y"]
    return out
